# version 26
# speedup vs baseline: 1.1834x; 1.0078x over previous
import numpy as np
import ml_dtypes
from contextlib import ExitStack

import concourse.bass as bass
import concourse.mybir as mybir
from concourse.bass_utils import run_bass_kernel_spmd

F32 = mybir.dt.float32
BF16 = mybir.dt.bfloat16
I32 = mybir.dt.int32
U8 = mybir.dt.uint8
AF = mybir.ActivationFunctionType
ALU = mybir.AluOpType
BF16NP = ml_dtypes.bfloat16

D = 1024
DH = 64
HT = 16
HPC = 4
H8 = 8
WIN = 512
HID = 2730
HIDP = 2816
NHC = HIDP // 128
HALO = 768
HB = 1024
TSB = 512
GROUPS = [[0, 1, 2, 3], [4, 5, 6, 7]]
EPS = 1e-5
CTOP = 8.0
SCALE_TOP = float(np.float32(np.float32(1024.0) ** np.float32(-0.05)))
QK_SCALE = 8.0

ENGINES = ("sync", "gpsimd", "tensor", "scalar", "vector")
ENG_SEM = {"sync": "LS", "gpsimd": "PO", "tensor": "PE", "scalar": "AC", "vector": "DV"}
DMA_SEM = {"sync": "LS", "gpsimd": "LG", "scalar": "LA", "vector": "LV"}
DMA_SEMS = ("LS", "LG", "LA", "LV")


class Cfg:
    def __init__(self, seq=8192, taps=(), phases=(0, 1, 2, 3)):
        self.SEQ = seq
        self.phases = set(phases)
        self.TPC = seq // 4
        self.BW = self.TPC + HALO
        self.NTS = seq // TSB
        self.NHALF = self.TPC // HB
        self.taps = set(taps)
        assert self.TPC % HB == 0



class Res:
    __slots__ = ("w", "r", "sem", "stream", "open")

    def __init__(self, stream=False):
        self.w = []
        self.r = []
        self.sem = None
        self.stream = stream
        self.open = False


def mkres(*shape, **kw):
    if not shape:
        return Res(**kw)
    return [mkres(*shape[1:], **kw) for _ in range(shape[0])]


class B:
    def __init__(self, nc):
        self.nc = nc
        self.q = {e: [] for e in ENGINES}
        self.cnt = {}
        self.sems = {}
        self.env = {}
        self.ctx = ExitStack()
        self.nops = 0
        self.ndsem = 0
        self.pending = {}
        self.fence_marks = []
        self.dyncache = {}
        self.env_bounds = {}

    def sem(self, name):
        if name not in self.sems:
            self.sems[name] = self.ctx.enter_context(self.nc.semaphore(name))
            self.cnt.setdefault(name, 0)
        return self.sems[name]

    def fresh_dsem(self):
        self.ndsem += 1
        return f"D{self.ndsem}"

    def set_fence(self, exclude=()):
        ex = {r.sem for r in exclude if r.sem is not None}
        self.fence_marks = [(None, s2, c2) for s2, c2 in self.cnt.items()
                            if s2 not in ex]

    def res(self, stream=False):
        r = Res(stream=stream)
        r.r = list(self.fence_marks)
        return r

    def chain(self, *olds, stream=False):
        r = Res(stream=stream)
        for o in olds:
            r.r.extend(o.w)
            r.r.extend(o.r)
        return r

    def mkres(self, *shape, **kw):
        if not shape:
            return self.res(**kw)
        return [self.mkres(*shape[1:], **kw) for _ in range(shape[0])]

    def now(self, name):
        return self.cnt.get(name, 0)

    def op(self, eng, fn, reads=(), writes=(), war=(), extra_wait=(), dma=False,
           coll=False, quiet=False, dma_groups=1):
        need = {}

        def add(recs):
            for (e2, s2, c2) in recs:
                need[s2] = max(need.get(s2, 0), c2)

        if dma or coll:
            assert not quiet and not war
            assert len(writes) == 1, "dma/collective ops must write exactly one Res"
            res = writes[0]
            step = 16 * dma_groups if dma else 1
            if res.stream and res.open:
                pass
            else:
                add(res.w)
                add(res.r)
                if res.sem is None or (res.stream and res.r):
                    res.sem = self.fresh_dsem()
                res.r = []
                if res.stream:
                    res.w = []
                res.open = True
            if res.sem is None:
                res.sem = self.fresh_dsem()
            sem_name = res.sem
        else:
            step = 1
            sem_name = ENG_SEM[eng]
            for w in list(writes) + list(war):
                add(w.w)
                add(w.r)
        for r in reads:
            add(r.w)
        for (s2, c2) in extra_wait:
            need[s2] = max(need.get(s2, 0), c2)
        waits = [(self.sem(s), c) for (s, c) in sorted(need.items()) if c > 0]

        if quiet:
            def thunk(e, fn=fn, waits=waits):
                for (sh, c) in waits:
                    e.wait_ge(sh, c)
                fn(e)
            self.q[eng].append(thunk)
            self.pending.setdefault(eng, []).extend(reads)
            self.nops += 1
            return None

        self.cnt[sem_name] = self.cnt.get(sem_name, 0) + step
        tgt = self.cnt[sem_name]
        semh = self.sem(sem_name)
        self.nops += 1

        def thunk(e, fn=fn, waits=waits, semh=semh, step=step):
            for (sh, c) in waits:
                e.wait_ge(sh, c)
            ins = fn(e) if fn is not None else None
            if ins is None:
                e.nop().then_inc(semh, step)
            else:
                ins.then_inc(semh, step)

        self.q[eng].append(thunk)
        marker = (eng, sem_name, tgt)
        for r in list(reads) + self.pending.pop(eng, []):
            r.r.append(marker)
            r.open = False
        if dma or coll:
            res = writes[0]
            if res.stream:
                res.w.append(marker)
            else:
                res.w = [marker]
        else:
            for w in writes:
                w.w = [marker]
                w.r = []
        return marker

    def dynoff(self, e, base, off):
        if off == 0:
            return self.env[base]
        return self.env[base] + off

    def dma(self, eng, out, in_, reads=(), writes=(), extra_wait=(), cond_env=None,
            groups=1, **kw):
        def fn(e, out=out, in_=in_, kw=kw, cond_env=cond_env):
            src = in_(e) if callable(in_) else in_
            dst = out(e) if callable(out) else out
            if cond_env is not None:
                key = ("cond", cond_env)
                if key not in self.dyncache:
                    self.dyncache[key] = self.env[cond_env] != 0
                kw = dict(kw, cond=self.dyncache[key])
            try:
                return e.dma_start(out=dst, in_=src, **kw)
            except Exception:
                import sys, traceback
                traceback.print_exc()
                print("DMA FAIL kw=", kw, " dst=", str(dst)[:100], " srcoff=", str(getattr(src, "offset", None))[:300], file=sys.stderr)
                raise
        return self.op(eng, fn, reads=reads, writes=writes, extra_wait=extra_wait,
                       dma=True, dma_groups=groups)

    def raw(self, eng, fn):
        self.q[eng].append(fn)

    def emit(self):
        with self.nc.Block() as block:
            for eng in ENGINES:
                thunks = self.q[eng]
                getattr(block, eng)(lambda e, thunks=thunks: [t(e) for t in thunks])


class Alloc:
    def __init__(self, mk, ncols, name):
        self.mk, self.n, self.off, self.name = mk, ncols, 0, name
        self.marks = []
        self.peak = 0

    def view(self, cols):
        assert self.off + cols <= self.n, f"{self.name} pool overflow: {self.off}+{cols}>{self.n}"
        o = self.off
        self.off += cols
        self.peak = max(self.peak, self.off)
        return self.mk(o, cols)

    def mark(self):
        self.marks.append(self.off)

    def reset(self):
        self.off = self.marks.pop()



def _chunk_cols(v):
    return np.ascontiguousarray(v.reshape(-1, 128).T.astype(np.float32))


def _rotary_tables(seq):
    half = np.arange(0, DH, 2, dtype=np.float64)
    inv_freq = 1.0 / 10000.0 ** (half / DH)
    t = np.arange(-HALO, seq, dtype=np.float64)
    freqs = t[:, None] * inv_freq[None, :]
    freqs = np.concatenate([freqs, freqs], -1)
    base = (half + 0.4 * DH) / (1.4 * DH)
    power = (t - seq // 2) / float(WIN // 2)
    scale = base[None, :] ** power[:, None]
    scale = np.concatenate([scale, scale], -1)
    cos, sin = np.cos(freqs), np.sin(freqs)
    sign = np.concatenate([-np.ones(32), np.ones(32)])
    csq = (cos * scale).T
    ssq = sign[:, None] * (sin * scale).T
    csk = (cos / scale).T
    ssk = sign[:, None] * (sin / scale).T
    return [np.asarray(a, np.float32) for a in (csq, ssq, csk, ssk)]


def _masks_top():
    m = np.zeros((4, 128, TSB), np.float32)
    p = np.arange(128)[:, None]
    f = np.arange(TSB)[None, :]
    for v in range(4):
        m[v] = (f >= 128 * v + p)
    return np.ascontiguousarray(np.concatenate(list(m), axis=1)).astype(BF16NP)


def _masks_loc(gi):
    p = np.arange(128)[:, None]
    f = np.arange(TSB)[None, :]
    own = np.zeros((8, 128, TSB), np.uint8)
    for j in range(8):
        r = 128 * j + p
        own[j] = ((r - 512 <= f) & (r >= f)).astype(np.uint8)
    first = own.copy()
    if gi == 0:
        first[:4] = 0
    edge = np.zeros((5, 128, 128), np.uint8)
    fe = np.arange(128)[None, :]
    for j in range(5):
        tk = -640 + 128 * j + p
        tq = -128 + fe
        edge[j] = ((tk <= tq) & (tk >= tq - 512)).astype(np.uint8)
    if gi == 0:
        edge[:] = 1
    pk = lambda a: np.ascontiguousarray(np.concatenate(list(a), axis=1))
    return pk(own), pk(first), pk(edge)


class WPack:
    def __init__(self):
        self.items = {}
        self.n = 0

    def add(self, name, shape):
        self.items[name] = (self.n, shape)
        self.n += int(np.prod(shape))

    def finalize(self, ranks=8, cols=8192):
        per_rank = -(-self.n // (ranks * cols))
        self.rows = ranks * per_rank
        self.cols = cols

    def build_flat(self, values):
        flat = np.zeros(self.rows * self.cols, np.float32)
        for name, (off, shape) in self.items.items():
            v = np.ascontiguousarray(values[name]).astype(np.float32).ravel()
            assert v.size == int(np.prod(shape)), name
            flat[off:off + v.size] = v
        return flat.reshape(self.rows, self.cols)

    def ap(self, wbf_all, name, r0, r1, c0, c1):
        off, shape = self.items[name]
        C = shape[1]
        return bass.AP(wbf_all, off + r0 * C + c0, [[C, r1 - r0], [1, c1 - c0]])

    def ap3(self, wbf_all, name, r0, nr_groups, c0, c1):
        off, shape = self.items[name]
        C = shape[1]
        return bass.AP(wbf_all, off + r0 * C + c0,
                       [[C, 128], [128 * C, nr_groups], [1, c1 - c0]])


def make_wpack():
    wp = WPack()
    wp.add("Wp", (D, D))
    for l in range(2):
        wp.add(f"wqk{l}", (8 * 8 * 128, 128))
        wp.add(f"wv{l}", (D, 512))
        wp.add(f"wo{l}", (512, D))
        wp.add(f"w1ag{l}", (NHC * 8 * 128, 256))
        wp.add(f"w2{l}", (HIDP, D))
    wp.finalize()
    return wp



class Prog:
    def __init__(self, cfg: Cfg, wpack: WPack):
        self.cfg = cfg
        self.wp = wpack
        self.nc = bass.Bass("TRN2", target_bir_lowering=False)
        self.b = B(self.nc)
        self.taps = {}
        self._declare_io()
        self._declare_sbuf()
        self._phase0()
        if 1 in cfg.phases:
            self._phase1()
        if 2 in cfg.phases:
            self._phase2()
        if 3 in cfg.phases:
            self._phase3()
        self._finish()
        self.b.emit()
        self.b.ctx.close()

    def _declare_io(self):
        nc, cfg, wp = self.nc, self.cfg, self.wp
        SEQ, TPC, BW = cfg.SEQ, cfg.TPC, cfg.BW
        dt = nc.dram_tensor
        self.x_ext = dt("x_ext", [BW, D], F32, kind="ExternalInput")
        self.wq_my = dt("wq_my", [D, HPC * DH], F32, kind="ExternalInput")
        self.wk_my = dt("wk_my", [D, HPC * DH], F32, kind="ExternalInput")
        self.wv_my = dt("wv_my", [D, HPC * DH], F32, kind="ExternalInput")
        self.wall_my = dt("wall_my", [wp.rows // 8, wp.cols], F32, kind="ExternalInput")
        self.consts_in = dt("consts", [128, 8], F32, kind="ExternalInput")
        self.ident_in = dt("ident", [128, 128], F32, kind="ExternalInput")
        self.ones_in = dt("ones_in", [128, 132], F32, kind="ExternalInput")
        self.e2_in = dt("e2_in", [2, 128], F32, kind="ExternalInput")
        self.e2qk_in = dt("e2qk_in", [8, 128], F32, kind="ExternalInput")
        self.pswap_in = dt("pswap_in", [128, 128], F32, kind="ExternalInput")
        self.bp_in = dt("bp_in", [128, 8], F32, kind="ExternalInput")
        self.b1_in = dt("b1_in", [128, 2 * 2 * NHC], F32, kind="ExternalInput")
        self.b2lgb_in = dt("b2lgb_in", [128, 2 * 3 * 8], F32, kind="ExternalInput")
        self.qsks_in = dt("qsks_in", [128, 2 * 4], F32, kind="ExternalInput")
        self.tabs_in = dt("tabs_in", [128, 4 * (BW - 128)], F32, kind="ExternalInput")
        self.mtop_in = dt("mtop_in", [128, 4 * TSB], BF16, kind="ExternalInput")
        self.mownb_in = dt("mownb_in", [128, 8 * TSB], BF16, kind="ExternalInput")
        self.medgeb_in = dt("medgeb_in", [128, 5 * 128], BF16, kind="ExternalInput")
        self.ofs_in = dt("ofs_in", [1, 8], I32, kind="ExternalInput")
        self.hhf_in = dt("hhf_in", [128, 1], F32, kind="ExternalInput")
        self.y_out = dt("y_out", [TPC, D], F32, kind="ExternalOutput")

        self.xfm_in = dt("xfm_in", [D, TPC], BF16)
        self.xfm_all = dt("xfm_all", [4 * D, TPC], BF16)
        self.wbf_in = dt("wbf_in", [wp.rows // 8, wp.cols], BF16)
        self.wbf_all = dt("wbf_all", [wp.rows, wp.cols], BF16)
        self.o_in = dt("o_in", [HPC * DH, SEQ], BF16)
        self.o_all = dt("o_all", [HT * DH, SEQ], BF16)
        self.tail_in = dt("tail_in", [D, HALO], BF16)
        self.tail_all = dt("tail_all", [4 * D, HALO], BF16)
        self.lo_dram = dt("lo_dram", [D, BW], BF16)
        self.hi_dram = dt("hi_dram", [D, BW], BF16)
        self.ubf_dram = dt("ubf_dram", [D, BW], BF16)
        self.u_dram = dt("u_dram", [D, TPC], F32)
        self.a2_dram = dt("a2_dram", [D, TPC], F32)
        self.a2bf_dram = dt("a2bf_dram", [D, TPC], BF16)

        for name, shape, dty in (("xfm", [4 * D, TPC], BF16), ("o", [HT * DH, SEQ], BF16),
                                 ("u", [D, TPC], F32), ("a2", [D, TPC], F32),
                                 ("osb", [D, BW], BF16), ("upf", [D, TSB], F32)):
            if name in self.cfg.taps:
                self.taps[name] = dt("tap_" + name, shape, dty, kind="ExternalOutput")

    def _declare_sbuf(self):
        nc, b, cfg = self.nc, self.b, self.cfg
        E = b.ctx.enter_context
        self.consts_sb = E(nc.sbuf_tensor("consts_sb", [128, 8], F32))
        self.id_sb = E(nc.sbuf_tensor("id_sb", [128, 128], F32))
        self.ones_sb = E(nc.sbuf_tensor("ones_sb", [128, 132], F32))
        self.e2_sb = E(nc.sbuf_tensor("e2_sb", [2, 128], F32))
        self.e2qk_sb = [E(nc.sbuf_tensor(f"e2qk{i}_sb", [2, 128], F32)) for i in range(4)]
        self.onesb_sb = E(nc.sbuf_tensor("onesb_sb", [128, 4], BF16))
        self.pswap_sb = E(nc.sbuf_tensor("pswap_sb", [128, 128], BF16))
        self.bp_sb = E(nc.sbuf_tensor("bp_sb", [128, 8], F32))
        self.b1_sb = E(nc.sbuf_tensor("b1_sb", [128, 2 * 2 * NHC], F32))
        self.b2lgb_sb = E(nc.sbuf_tensor("b2lgb_sb", [128, 2 * 3 * 8], F32))
        self.qsks_sb = E(nc.sbuf_tensor("qsks_sb", [128, 2 * 4], F32))
        self.mtop_sb = E(nc.sbuf_tensor("mtop_sb", [128, 4 * TSB], BF16))
        self.mownb_sb = E(nc.sbuf_tensor("mownb_sb", [128, 8 * TSB], BF16))
        self.medgeb_sb = E(nc.sbuf_tensor("medgeb_sb", [128, 5 * 128], BF16))
        self.ofs_sb = E(nc.sbuf_tensor("ofs_sb", [1, 8], I32))
        self.hhf_sb = E(nc.sbuf_tensor("hhf_sb", [128, 1], F32))

        for i, v in enumerate([0.0, 1.0, -CTOP, EPS, 1.0 / D, -1.0, 1e-12, 0.5]):
            nc.const_aps.register(F32, float(v), self.consts_sb[:, i:i + 1])

        self.S0 = E(nc.psum_tensor("S0", [128, 1024], F32))
        self.S1 = E(nc.psum_tensor("S1", [128, 1024], F32))
        self.OA = E(nc.psum_tensor("OA", [128, 512], F32))
        self.OB = E(nc.psum_tensor("OB", [128, 512], F32))
        self.PX = E(nc.psum_tensor("PX", [128, 1024], F32))
        self.rS0, self.rS1, self.rOA, self.rOB, self.rPX = (Res() for _ in range(5))

        self.NBF = 65 * 1024
        self.NF32 = 14 * 1024
        pbf = E(nc.sbuf_tensor("pool_bf", [128, self.NBF], BF16))
        pf32 = E(nc.sbuf_tensor("pool_f32", [128, self.NF32], F32))
        self.abf = Alloc(lambda o, n: pbf[:, o:o + n], self.NBF, "bf")
        self.af32 = Alloc(lambda o, n: pf32[:, o:o + n], self.NF32, "f32")

    def _phase0(self):
        b, nc, cfg, wp = self.b, self.nc, self.cfg, self.wp
        TPC, BW = cfg.TPC, cfg.BW
        self.rConsts = Res(stream=True)
        self.rConstsG = Res()
        self.rZero = Res()
        for dst, src in ((self.consts_sb, self.consts_in), (self.id_sb, self.ident_in),
                         (self.ones_sb, self.ones_in), (self.bp_sb, self.bp_in),
                         (self.b1_sb, self.b1_in), (self.b2lgb_sb, self.b2lgb_in),
                         (self.qsks_sb, self.qsks_in), (self.mtop_sb, self.mtop_in),
                         (self.ofs_sb, self.ofs_in),
                         (self.mownb_sb, self.mownb_in), (self.medgeb_sb, self.medgeb_in)):
            b.dma("sync", dst[:, :], src[:, :], writes=[self.rConsts])
        b.dma("sync", self.e2_sb[:, :], self.e2_in[:, :], writes=[self.rConsts])
        for i in range(4):
            b.dma("sync", self.e2qk_sb[i][:, :], self.e2qk_in[2 * i:2 * i + 2, :],
                  writes=[self.rConsts])
        self.rConstsG = Res()
        b.dma("sync", self.hhf_sb[:, :], self.hhf_in[:, :], writes=[self.rConsts])
        b.dma("gpsimd", self.pswap_sb[:, :], self.pswap_in[:, :], writes=[self.rConstsG])

        b.op("vector", lambda e: e.tensor_copy(out=self.onesb_sb[:, :], in_=self.ones_sb[:, 0:4]),
             reads=[self.rConsts], writes=[self.rZero])

        def setup_regs(e):
            names = ["t0", "halo0", "xrow_halo"]
            SEQ = cfg.SEQ
            lims = [(0, SEQ - TPC), (0, SEQ - TPC), (0, 3 * D)]
            for i, (nm, (lo, hi)) in enumerate(zip(names, lims)):
                r = e.alloc_register()
                e.reg_load(r, self.ofs_sb[0:1, i:i + 1])
                b.env[nm] = e.snap(r, min_val=lo, max_val=hi)
                b.env_bounds[nm] = (lo, hi)
            rb = e.alloc_register()
            e.reg_mul(rb, b.env["xrow_halo"], HALO)
            b.env["xrow_halo_b"] = e.snap(rb, min_val=0, max_val=3 * D * HALO)
        b.op("sync", None, reads=[self.rConsts])
        b.raw("sync", setup_regs)

        def setup_regs_sc(e):
            r = e.alloc_register()
            e.reg_load(r, self.ofs_sb[0:1, 2:3])
            rb = e.alloc_register()
            e.reg_mul(rb, e.snap(r, min_val=0, max_val=3 * D), HALO)
            b.env["sc_xrow_halo_b"] = e.snap(rb, min_val=0, max_val=3 * D * HALO)
        b.op("scalar", None, reads=[self.rConsts])
        b.raw("scalar", setup_regs_sc)

        self.wqh = [self.abf.view(8 * 256) for _ in range(3)]
        self.rWqh = Res()
        for w_dram, sb in ((self.wq_my, self.wqh[0]), (self.wk_my, self.wqh[1]),
                           (self.wv_my, self.wqh[2])):
            b.dma("gpsimd", sb.rearrange("p (c d) -> p c d", c=8),
                  bass.AP(w_dram, 0, [[256, 128], [128 * 256, 8], [1, 256]]),
                  writes=[self.rWqh], groups=8)

        self.abf.mark(); self.af32.mark()
        NXT = BW // 128
        OWN0 = HALO // 128
        xtile = [self.af32.view(1024) for _ in range(2)]
        xfmf = [self.af32.view(1024) for _ in range(2)]
        hist = [self.abf.view(1024) for _ in range(2)]
        lost = [self.abf.view(1024) for _ in range(2)]
        rXt, rXf, rHi, rLo = mkres(2), mkres(2), mkres(2), mkres(2)
        self.rXfmIn, self.rLoDram = mkres(2, stream=True), mkres(2, stream=True)
        self.rHiDram = mkres(2, stream=True)
        PX, rPX = self.PX, self.rPX
        for i in range(NXT):
            bi = i % 2
            b.dma("sync" if bi == 0 else "scalar", xtile[bi][:, :],
                  self.x_ext[i * 128:(i + 1) * 128, :],
                  writes=[rXt[bi]])
            for j in range(8):
                b.op("tensor",
                     lambda e, bi=bi, j=j: e.transpose(PX[:, j * 128:(j + 1) * 128],
                                                       xtile[bi][:, j * 128:(j + 1) * 128],
                                                       self.id_sb[:, :]),
                     reads=[rXt[bi], self.rConsts],
                     war=[rPX] if j == 0 else (),
                     writes=[rPX] if j == 7 else (), quiet=(j < 7))
            b.op("scalar", lambda e, bi=bi: e.copy(xfmf[bi][:, :], PX[:, :]),
                 reads=[rPX], writes=[rXf[bi]])
            b.op("vector", lambda e, bi=bi: e.tensor_copy(out=hist[bi][:, :], in_=xfmf[bi][:, :]),
                 reads=[rXf[bi]], writes=[rHi[bi]])
            b.op("vector", lambda e, bi=bi: e.tensor_tensor(lost[bi][:, :], xfmf[bi][:, :],
                                                            hist[bi][:, :], ALU.subtract),
                 reads=[rXf[bi], rHi[bi]], writes=[rLo[bi]])
            if i >= OWN0:
                toff = (i - OWN0) * 128
                b.dma("sync", bass.AP(self.xfm_in, toff, [[TPC, 128], [128 * TPC, 8], [1, 128]]),
                      hist[bi].rearrange("p (j t) -> p j t", j=8),
                      reads=[rHi[bi]], writes=[self.rXfmIn[bi]], groups=8)
            b.dma("scalar", bass.AP(self.hi_dram, i * 128, [[BW, 128], [128 * BW, 8], [1, 128]]),
                  hist[bi].rearrange("p (j t) -> p j t", j=8),
                  reads=[rHi[bi]], writes=[self.rHiDram[bi]], groups=8)
            b.dma("scalar", bass.AP(self.lo_dram, i * 128, [[BW, 128], [128 * BW, 8], [1, 128]]),
                  lost[bi].rearrange("p (j t) -> p j t", j=8),
                  reads=[rLo[bi]], writes=[self.rLoDram[bi]], groups=8)

        import os
        self.rXfmAll, self.rWbfAll = Res(), Res()
        if not os.environ.get("NOAG1"):
            b.op("gpsimd",
                 lambda e: e.collective_compute("AllGather", ALU.bypass, replica_groups=GROUPS,
                                                ins=[self.xfm_in.ap().opt()],
                                                outs=[self.xfm_all.ap().opt()]),
                 reads=self.rXfmIn, writes=[self.rXfmAll], coll=True)
        else:
            self.rXfmAll = self.rXfmIn

        WR = wp.rows // 8
        self.rWbfIn = Res()
        for r0 in range(0, WR, 128):
            r1 = min(r0 + 128, WR)
            b.dma("gpsimd", self.wbf_in[r0:r1, :], self.wall_my[r0:r1, :],
                  writes=[self.rWbfIn])
        if not os.environ.get("NOAGW"):
            b.op("gpsimd",
                 lambda e: e.collective_compute("AllGather", ALU.bypass,
                                                replica_groups=[list(range(8))],
                                                ins=[self.wbf_in.ap().opt()],
                                                outs=[self.wbf_all.ap().opt()]),
                 reads=[self.rWbfIn], writes=[self.rWbfAll], coll=True)
        else:
            self.rWbfAll = self.rWbfIn
        self.abf.reset(); self.af32.reset()
        ex = [r for r in (self.rXfmAll, self.rWbfAll, self.rWbfIn) if isinstance(r, Res)]
        b.set_fence(exclude=ex)

    def _phase1(self):
        b, cfg = self.b, self.cfg
        SEQ, TPC, NTS = cfg.SEQ, cfg.TPC, cfg.NTS
        PX, S0, S1, OA, OB = self.PX, self.S0, self.S1, self.OA, self.OB
        rPX, rS0, rS1, rOA, rOB = self.rPX, self.rS0, self.rS1, self.rOA, self.rOB
        self.abf.mark(); self.af32.mark()
        q_strip = [self.abf.view(TSB) for _ in range(2)]
        k_strip = [self.abf.view(SEQ) for _ in range(2)]
        v_strip = self.abf.view((SEQ // 128) * 260)
        xst = [self.abf.view(8 * TSB) for _ in range(2)]
        wqh = self.wqh
        pt = [self.abf.view(1024) for _ in range(2)]
        o_stage = [self.abf.view(512) for _ in range(2)]
        ot = [self.af32.view(512) for _ in range(2)]
        dinv = self.af32.view(1024)

        rQs, rKs, rXst, rPt, rOst, rOt = (b.mkres(2) for _ in range(6))
        rV, rDinv = b.res(), b.res()
        rWqh = self.rWqh
        rOin = b.mkres(2, stream=True)

        b.op("vector", lambda e: e.memset(v_strip, 1.0), writes=[rV])

        def mk_tail1(hp, TS):
            tgt, rtgt = (PX, rPX) if hp == 0 else (S0, rS0)

            def go():
                for h in range(2):
                    b.op("vector",
                         lambda e, h=h: e.reciprocal(dinv[0:1, h * 512:(h + 1) * 512],
                                                     ot[h][64:65, :]),
                         reads=[rOt[h]], writes=[rDinv])
                b.op("tensor",
                     lambda e, tgt=tgt: e.matmul(tgt[0:64, 0:512], self.ones_sb[0:1, 0:64],
                                                 dinv[0:1, 0:512], start=True, stop=True),
                     reads=[rDinv, self.rConsts], war=[rtgt], quiet=True)
                b.op("tensor",
                     lambda e, tgt=tgt: e.matmul(tgt[0:64, 512:1024],
                                                 self.ones_sb[0:1, 0:64],
                                                 dinv[0:1, 512:1024], start=True, stop=True),
                     reads=[rDinv], writes=[rtgt])
                for h in range(2):
                    b.op("vector",
                         lambda e, h=h, tgt=tgt:
                         e.tensor_tensor(o_stage[h][0:64, :], ot[h][0:64, :],
                                         tgt[0:64, h * 512:(h + 1) * 512], ALU.mult),
                         reads=[rOt[h], rtgt], writes=[rOst[h]])
                    b.dma("sync",
                          bass.AP(self.o_in, (hp * 2 + h) * 64 * SEQ + TS * TSB,
                                  [[SEQ, 64], [1, TSB]]),
                          o_stage[h][0:64, :], reads=[rOst[h]], writes=[rOin[h]])
            return go

        pend1 = []
        for TS in range(NTS):
            bi = TS % 2
            g = TS * TSB
            rank, tl = g // TPC, g % TPC
            b.dma("sync", xst[bi].rearrange("p (c t) -> p c t", c=8),
                  bass.AP(self.xfm_all, rank * D * TPC + tl,
                          [[TPC, 128], [128 * TPC, 8], [1, TSB]]),
                  reads=[self.rXfmAll], writes=[rXst[bi]], groups=8)
            for wsb, strips, rstrips, isq in ((wqh[0], q_strip, rQs, True),
                                              (wqh[1], k_strip, rKs, False)):
                for hp in range(2):
                    for c in range(8):
                        last = (hp == 1 and c == 7)
                        b.op("tensor",
                             lambda e, wsb=wsb, c=c, hp=hp, bi=bi:
                             e.matmul(PX[:, hp * 512:hp * 512 + 512],
                                      wsb[:, c * 256 + hp * 128:c * 256 + hp * 128 + 128],
                                      xst[bi][:, c * TSB:(c + 1) * TSB],
                                      start=(c == 0), stop=(c == 7)),
                             reads=[rWqh, rXst[bi]],
                             war=[rPX] if (hp == 0 and c == 0) else (),
                             writes=[rPX] if last else (), quiet=not last)
                for hp in range(2):
                    dst0 = 0 if isq else TS * TSB
                    b.op("scalar",
                         lambda e, strips=strips, hp=hp, dst0=dst0:
                         e.copy(strips[hp][:, dst0:dst0 + TSB],
                                PX[:, hp * 512:hp * 512 + 512]),
                         reads=[rPX], writes=[rstrips[hp]])
            if pend1:
                pend1.pop(0)()
            for tt in range(4):
                for c in range(8):
                    b.op("tensor",
                         lambda e, c=c, tt=tt, bi=bi:
                         e.matmul(PX[:, 0:256],
                                  xst[bi][:, c * TSB + tt * 128:c * TSB + tt * 128 + 128],
                                  wqh[2][:, c * 256:(c + 1) * 256],
                                  start=(c == 0), stop=(c == 7)),
                         reads=[rWqh, rXst[bi]], war=[rPX] if c == 0 else (),
                         writes=[rPX] if c == 7 else (), quiet=(c < 7))
                vt = TS * 4 + tt
                b.op("vector",
                     lambda e, vt=vt:
                     e.tensor_copy(out=v_strip[:, vt * 260:vt * 260 + 260]
                                   .rearrange("p (h x) -> p h x", x=65)[:, :, 0:64],
                                   in_=PX[:, 0:256].rearrange("p (h d) -> p h d", d=64)),
                     reads=[rPX], writes=[rV])
            n_chunks = 4 * TS + 4

            def emit_av(j, hp):
                lastj = (j == n_chunks - 1)
                for h in range(2):
                    tgt, rt = (OA, rOA) if h == 0 else (OB, rOB)
                    b.op("tensor",
                         lambda e, j=j, h=h, hp=hp, tgt=tgt, n=n_chunks:
                         e.matmul(tgt[0:65, :],
                                  v_strip[:, j * 260 + (hp * 2 + h) * 65:
                                          j * 260 + (hp * 2 + h) * 65 + 65],
                                  pt[j % 2][:, h * 512:(h + 1) * 512],
                                  start=(j == 0), stop=(j == n - 1)),
                         reads=[rPt[j % 2], rV], war=[rt] if j == 0 else (),
                         writes=[rt] if lastj else (), quiet=not lastj)

            for hp in range(2):
                for j in range(n_chunks):
                    ps, rps = (S0, rS0) if j % 2 == 0 else (S1, rS1)
                    for h in range(2):
                        b.op("tensor",
                             lambda e, hp=hp, j=j, h=h, TS=TS, ps=ps:
                             e.matmul(ps[:, h * 512:(h + 1) * 512],
                                      k_strip[hp][h * 64:(h + 1) * 64, j * 128:(j + 1) * 128],
                                      q_strip[hp][h * 64:(h + 1) * 64, 0:TSB],
                                      start=True, stop=True, tile_position=(h * 64, 0)),
                             reads=[rKs[hp], rQs[hp]], war=[rps] if h == 0 else (),
                             writes=[rps] if h == 1 else (), quiet=(h == 0))
                    b.op("scalar",
                         lambda e, j=j, ps=ps:
                         e.activation(pt[j % 2][:, :], ps[:, :], AF.Exp,
                                      bias=-CTOP, scale=SCALE_TOP),
                         reads=[rps, self.rConsts], writes=[rPt[j % 2]])
                    if j >= 4 * TS:
                        v = j - 4 * TS
                        for h in range(2):
                            b.op("gpsimd",
                                 lambda e, j=j, v=v, h=h:
                                 e.tensor_tensor(pt[j % 2][:, h * 512:(h + 1) * 512],
                                                 pt[j % 2][:, h * 512:(h + 1) * 512],
                                                 self.mtop_sb[:, v * TSB:(v + 1) * TSB],
                                                 ALU.mult),
                                 reads=[self.rConsts], writes=[rPt[j % 2]])
                    if j >= 1:
                        emit_av(j - 1, hp)
                    if hp == 1 and j == 2 and pend1:
                        pend1.pop(0)()
                emit_av(n_chunks - 1, hp)
                for h, (ps, rps) in enumerate(((OA, rOA), (OB, rOB))):
                    b.op("scalar", lambda e, h=h, ps=ps: e.copy(ot[h][0:65, :], ps[0:65, :]),
                         reads=[rps], writes=[rOt[h]])
                pend1.append(mk_tail1(hp, TS))
        while pend1:
            pend1.pop(0)()

        self.rOAll = Res()
        b.op("gpsimd",
             lambda e: e.collective_compute("AllGather", ALU.bypass, replica_groups=GROUPS,
                                            ins=[self.o_in.ap().opt()],
                                            outs=[self.o_all.ap().opt()]),
             reads=rOin, writes=[self.rOAll], coll=True)
        self.abf.reset(); self.af32.reset()
        self.abf.off = 0
        b.set_fence()

    def _phase2(self):
        b, cfg, wp = self.b, self.cfg, self.wp
        SEQ, TPC, BW = cfg.SEQ, cfg.TPC, cfg.BW
        PX, OA, OB, S1 = self.PX, self.OA, self.OB, self.S1
        self.abf.mark(); self.af32.mark()
        o3 = [self.abf.view(8 * TSB) for _ in range(3)]
        hi2 = [self.abf.view(8 * TSB) for _ in range(3)]
        lo2 = [self.abf.view(8 * TSB) for _ in range(3)]
        upbf = self.abf.view(8 * TSB)
        sqbf = [self.abf.view(TSB) for _ in range(2)]
        ubst = [self.abf.view(8 * TSB) for _ in range(2)]
        wp_sb = self.abf.view(8 * 1024)
        upf = [self.af32.view(TSB) for _ in range(8)]
        ufst = [self.af32.view(8 * TSB) for _ in range(2)]
        mrow = self.af32.view(2 * TSB)
        srow = self.af32.view(2 * TSB)

        rO3 = b.mkres(3)
        rO3s = b.mkres(3)
        rHi2, rLo2 = b.mkres(3), b.mkres(3)
        rUp = b.mkres(8)
        rUbst, rUfst = b.mkres(2), b.mkres(2)
        rUpbf, rWp = (b.res() for _ in range(2))
        rSq = b.mkres(2)
        rMrow, rSrow = b.res(), b.res()
        rOA2, rOB2, rS12 = b.res(), b.res(), b.res()
        rPXh = b.mkres(2)

        for c in range(8):
            b.dma("sync", wp_sb[:, c * 1024:(c + 1) * 1024],
                  wp.ap(self.wbf_all, "Wp", 128 * c, 128 * c + 128, 0, D),
                  reads=[self.rWbfAll], writes=[rWp])

        import os
        blocks = [(k * 256, 256, False) for k in range(HALO // 256)] + \
                 [(HALO + k * 512, 512, True) for k in range(TPC // 512)]
        if os.environ.get("P2BLOCKS"):
            sel = [int(z) for z in os.environ["P2BLOCKS"].split(",")]
            blocks = [blocks[i] for i in sel]
        NB = len(blocks)
        self.rUdram = b.mkres(2)
        self.rUbfD = b.mkres(2)

        def emit_loads(ib):
            if ib >= NB:
                return
            c0, wdt, own = blocks[ib]
            bi = ib % 3
            base, off = ("t0", c0 - HALO) if own else ("halo0", c0)
            b.dma("sync",
                  hi2[bi].rearrange("p (c t) -> p c t", c=8)[:, :, 0:wdt],
                  bass.AP(self.hi_dram, c0, [[BW, 128], [128 * BW, 8], [1, wdt]]),
                  reads=self.rHiDram, writes=[rHi2[bi]], groups=8)
            b.dma("scalar",
                  lo2[bi].rearrange("p (c t) -> p c t", c=8)[:, :, 0:wdt],
                  bass.AP(self.lo_dram, c0, [[BW, 128], [128 * BW, 8], [1, wdt]]),
                  reads=self.rLoDram, writes=[rLo2[bi]], groups=8)
            b.dma("sync",
                  o3[bi].rearrange("p (c t) -> p c t", c=8)[:, :, 0:wdt],
                  lambda e, base=base, off=off, wdt=wdt:
                  bass.AP(self.o_all, b.env[base] + off,
                          [[SEQ, 128], [128 * SEQ, 8], [1, wdt]]),
                  reads=[self.rOAll], writes=[rO3[bi]], groups=8)
            if not own:
                b.op("vector",
                     lambda e, bi=bi, wdt=wdt: e.tensor_scalar_mul(
                         o3[bi].rearrange("p (c t) -> p c t", c=8)[:, :, 0:wdt],
                         o3[bi].rearrange("p (c t) -> p c t", c=8)[:, :, 0:wdt],
                         self.hhf_sb[:, 0:1]),
                     reads=[self.rConsts, rO3[bi]], writes=[rO3s[bi]])

        def emit_stats(oc, wdt):
            b.op("tensor",
                 lambda e, oc=oc, wdt=wdt:
                 e.matmul(OA[0:1, 0:wdt], self.onesb_sb[:, 1:2],
                          upbf[:, oc * TSB:oc * TSB + wdt],
                          start=(oc == 0), stop=(oc == 7)),
                 reads=[rUpbf], war=[rOA2] if oc == 0 else (),
                 writes=[rOA2])
            b.op("tensor",
                 lambda e, oc=oc, wdt=wdt:
                 e.matmul(OB[0:1, 0:wdt], self.onesb_sb[:, 1:2],
                          sqbf[oc % 2][:, 0:wdt],
                          start=(oc == 0), stop=(oc == 7)),
                 reads=[rSq[oc % 2]], war=[rOB2] if oc == 0 else (),
                 writes=[rOB2])

        def mk_tail(ib):
            c0, wdt, own = blocks[ib]
            bi = ib % 2
            uf, ub = ufst[bi], ubst[bi]

            def go():
                self._ln_rows(wdt, rOA2, rOB2, rS12, mrow, srow, rMrow, rSrow)
                for oc in range(8):
                    b.op("vector",
                         lambda e, oc=oc, wdt=wdt:
                         e.tensor_tensor(upf[oc][:, 0:wdt], upf[oc][:, 0:wdt],
                                         S1[:, 0:wdt], ALU.mult),
                         reads=[rS12, rUp[oc]], writes=[rUp[oc]])
                    b.op("vector",
                         lambda e, oc=oc, wdt=wdt:
                         e.tensor_tensor(uf[:, oc * TSB:oc * TSB + wdt],
                                         upf[oc][:, 0:wdt],
                                         S1[:, 512:512 + wdt], ALU.add),
                         reads=[rS12, rUp[oc]], writes=[rUfst[bi]])
                    b.op("vector",
                         lambda e, oc=oc, wdt=wdt:
                         e.tensor_copy(out=ub[:, oc * TSB:oc * TSB + wdt],
                                       in_=uf[:, oc * TSB:oc * TSB + wdt]),
                         reads=[rUfst[bi]], writes=[rUbst[bi]])
                if not own:
                    b.op("vector",
                         lambda e, wdt=wdt: e.tensor_scalar_mul(
                             ub.rearrange("p (j t) -> p j t", j=8)[:, :, 0:wdt],
                             ub.rearrange("p (j t) -> p j t", j=8)[:, :, 0:wdt],
                             self.hhf_sb[:, 0:1]),
                         reads=[self.rConsts, rUbst[bi]], writes=[rUbst[bi]])
                b.dma("scalar",
                      bass.AP(self.ubf_dram, c0, [[BW, 128], [128 * BW, 8], [1, wdt]]),
                      ub.rearrange("p (j t) -> p j t", j=8)[:, :, 0:wdt],
                      reads=[rUbst[bi]], writes=[self.rUbfD[ib % 2]], groups=8)
                if own:
                    b.dma("gpsimd",
                          bass.AP(self.u_dram, c0 - HALO,
                                  [[TPC, 128], [128 * TPC, 8], [1, wdt]]),
                          uf.rearrange("p (j t) -> p j t", j=8)[:, :, 0:wdt],
                          reads=[rUfst[bi]], writes=[self.rUdram[ib % 2]], groups=8)
            return go

        emit_loads(0)
        emit_loads(1)
        emit_loads(2)
        pend = []
        for ib, (c0, wdt, own) in enumerate(blocks):
            bi = ib % 3
            for oc in range(8):
                half = oc % 2
                for c in range(8):
                    last = c == 7
                    b.op("tensor",
                         lambda e, oc=oc, c=c, bi=bi, half=half, wdt=wdt:
                         e.matmul(PX[:, half * 512:half * 512 + wdt],
                                  wp_sb[:, c * 1024 + oc * 128:c * 1024 + oc * 128 + 128],
                                  o3[bi][:, c * TSB:c * TSB + wdt],
                                  start=(c == 0), stop=(c == 7)),
                         reads=[rWp, rO3[bi], rO3s[bi]],
                         war=[rPXh[half]] if c == 0 else (),
                         writes=[rPXh[half]] if last else (), quiet=not last)
                if oc == 0 and pend:
                    pend.pop(0)()
                if oc >= 2:
                    emit_stats(oc - 2, wdt)
                b.op("scalar",
                     lambda e, oc=oc, half=half, wdt=wdt:
                     e.activation(upf[oc][:, 0:wdt], PX[:, half * 512:half * 512 + wdt],
                                  AF.Identity, bias=self.bp_sb[:, oc:oc + 1], scale=1.0),
                     reads=[rPXh[half], self.rConsts], writes=[rUp[oc]])
                if ib == 0 and "upf" in self.taps:
                    b.dma("sync", self.taps["upf"][128 * oc:128 * oc + 128, 0:wdt],
                          upf[oc][:, 0:wdt], reads=[rUp[oc]], writes=[Res()])
                b.op("vector",
                     lambda e, oc=oc, bi=bi, wdt=wdt:
                     e.tensor_tensor(upf[oc][:, 0:wdt], upf[oc][:, 0:wdt],
                                     hi2[bi][:, oc * TSB:oc * TSB + wdt], ALU.add),
                     reads=[rHi2[bi], rUp[oc]], writes=[rUp[oc]])
                b.op("vector",
                     lambda e, oc=oc, bi=bi, wdt=wdt:
                     e.tensor_tensor(upf[oc][:, 0:wdt], upf[oc][:, 0:wdt],
                                     lo2[bi][:, oc * TSB:oc * TSB + wdt], ALU.add),
                     reads=[rLo2[bi], rUp[oc]], writes=[rUp[oc]])
                b.op("vector",
                     lambda e, oc=oc, wdt=wdt:
                     e.tensor_copy(out=upbf[:, oc * TSB:oc * TSB + wdt],
                                   in_=upf[oc][:, 0:wdt]),
                     reads=[rUp[oc]], writes=[rUpbf])
                b.op("gpsimd",
                     lambda e, oc=oc, wdt=wdt:
                     e.tensor_tensor(sqbf[oc % 2][:, 0:wdt],
                                     upbf[:, oc * TSB:oc * TSB + wdt],
                                     upbf[:, oc * TSB:oc * TSB + wdt], ALU.mult),
                     reads=[rUpbf], writes=[rSq[oc % 2]])
            emit_stats(6, wdt)
            emit_stats(7, wdt)
            emit_loads(ib + 3)
            pend.append(mk_tail(ib))
        while pend:
            pend.pop(0)()

        self.abf.reset(); self.af32.reset()
        b.set_fence()

    def _ln_rows(self, wdt, rOA2, rOB2, rS12, mrow, srow, rMrow, rSrow):
        b = self.b
        OA, OB, S1 = self.OA, self.OB, self.S1
        b.op("vector",
             lambda e: e.tensor_scalar_mul(mrow[0:1, 0:wdt], OA[0:1, 0:wdt], 1.0 / D),
             reads=[rOA2], writes=[rMrow])
        b.op("vector",
             lambda e: e.scalar_tensor_tensor(srow[0:1, 0:wdt], OA[0:1, 0:wdt],
                                              1.0 / D, mrow[0:1, 0:wdt],
                                              ALU.mult, ALU.mult),
             reads=[rOA2, rMrow], writes=[rSrow])
        b.op("vector",
             lambda e: e.scalar_tensor_tensor(mrow[0:1, 512:512 + wdt], OB[0:1, 0:wdt],
                                              1.0 / D, srow[0:1, 0:wdt],
                                              ALU.mult, ALU.subtract),
             reads=[rOB2, rSrow], writes=[rMrow])
        b.op("scalar",
             lambda e: e.activation(srow[0:1, 0:wdt], mrow[0:1, 512:512 + wdt],
                                    AF.Ln, bias=EPS, scale=1.0),
             reads=[rMrow, self.rConsts], writes=[rSrow])
        b.op("scalar",
             lambda e: e.activation(srow[0:1, 0:wdt], srow[0:1, 0:wdt],
                                    AF.Exp, bias=0.0, scale=-0.5),
             reads=[rSrow, self.rConsts], writes=[rSrow])
        b.op("vector",
             lambda e: e.scalar_tensor_tensor(srow[0:1, 512:512 + wdt], mrow[0:1, 0:wdt],
                                              -1.0, srow[0:1, 0:wdt],
                                              ALU.mult, ALU.mult),
             reads=[rMrow, rSrow], writes=[rSrow])
        b.op("tensor",
             lambda e: e.matmul(S1[:, 0:wdt], self.ones_sb[0:1, 0:128],
                                srow[0:1, 0:wdt], start=True, stop=True),
             reads=[rSrow, self.rConsts], war=[rS12], quiet=True)
        b.op("tensor",
             lambda e: e.matmul(S1[:, 512:512 + wdt], self.ones_sb[0:1, 0:128],
                                srow[0:1, 512:512 + wdt], start=True, stop=True),
             reads=[rSrow, self.rConsts], writes=[rS12])

    def _phase3(self):
        b, cfg, wp = self.b, self.cfg, self.wp
        SEQ, TPC, BW = cfg.SEQ, cfg.TPC, cfg.BW
        PX, OA, OB, S0, S1 = self.PX, self.OA, self.OB, self.S0, self.S1
        NH = TPC // 512
        self.abf.mark(); self.af32.mark()

        awin = [self.abf.view(1153) for _ in range(8)]
        wqks = [self.abf.view(1024) for _ in range(2)]
        wv_sb = self.abf.view(8 * 512)
        wo_sb = self.abf.view(4 * 1024)
        qstr = [self.abf.view(640) for _ in range(4)]
        kstr = [self.abf.view(1152) for _ in range(4)]
        v3 = self.abf.view(9 * 520)
        tq1 = self.abf.view(640); tq2 = self.abf.view(640)
        tk1 = self.abf.view(1152); tk2 = self.abf.view(1152)
        hln = [self.abf.view(512) for _ in range(8)]
        gg = self.abf.view(NHC * 512)
        pt3 = [self.abf.view(1024) for _ in range(2)]
        ptsel = [self.abf.view(1024) for _ in range(2)]
        olq = [self.abf.view(512) for _ in range(4)]
        qnp = [self.abf.view(512) for _ in range(2)]
        sqp = [self.abf.view(512) for _ in range(2)]
        rotf = [self.abf.view(512) for _ in range(2)]
        w1s = [self.abf.view(2048) for _ in range(2)]
        w2s = [self.abf.view(1024) for _ in range(2)]
        a2bst = self.abf.view(512)
        xm = [self.af32.view(513) for _ in range(8)]
        scr1024 = self.af32.view(1024)
        xme = [scr1024[:, i * 128:(i + 1) * 128] for i in range(8)]
        xedge = self.af32.view(8)
        af = [self.af32.view(512) for _ in range(8)]
        gaf = self.af32.view(512)
        ggf = self.af32.view(512)
        ytm = scr1024
        tabst = self.af32.view(512)
        otq = [self.af32.view(512) for _ in range(2)]
        mrow3 = self.af32.view(1024)
        srow3 = self.af32.view(1024)

        rWqks = b.mkres(2)
        rWv, rWo = b.mkres(2), b.mkres(2)
        rAwin = b.mkres(8)
        rQs3, rKs3 = b.mkres(4), b.mkres(4)
        rV3 = b.res()
        rT = b.res()
        rHln = b.mkres(8)
        rGg = b.res()
        rPt3 = b.mkres(2)
        rPtselA, rPtselB = b.mkres(2), b.mkres(2)
        rOlq = b.mkres(4)
        rQnp, rSqp = b.mkres(2), b.mkres(2)
        rMr, rSr, rRot = b.mkres(2), b.mkres(2), b.mkres(2)
        rW1s, rW2s = b.mkres(2), b.mkres(2)
        rA2bst = b.res()
        rXm = b.mkres(8)
        rScr = b.res()
        rXme = [rScr] * 8
        rXedge = b.res()
        rAf = b.mkres(8)
        rGaf, rGgf, rTabst = b.res(), b.res(), b.res()
        rYtm = rScr
        rOtq = b.mkres(2)
        rDinv3, rMrow3, rSrow3 = b.res(), b.res(), b.res()
        rS03, rS13, rOA3, rOB3 = b.res(), b.res(), b.res(), b.res()
        rPXh3 = b.mkres(2)
        rTailIn = b.res(stream=True)
        rTailAll = b.res()
        self.rA2dram = b.mkres(8, stream=True)
        rA2bf = b.mkres(8, stream=True)
        rY = b.res(stream=True)

        for l in range(2):
            for c in range(8):
                b.dma("sync", wv_sb[:, c * 512:(c + 1) * 512],
                      wp.ap(self.wbf_all, f"wv{l}", 128 * c, 128 * c + 128, 0, 512),
                      reads=[self.rWbfAll], writes=[rWv[c % 2]])
            for ch in range(4):
                b.dma("sync", wo_sb[:, ch * 1024:(ch + 1) * 1024],
                      wp.ap(self.wbf_all, f"wo{l}", 128 * ch, 128 * ch + 128, 0, D),
                      reads=[self.rWbfAll], writes=[rWo[ch % 2]])

            for h in range(NH):
                OH0 = HALO + 512 * h
                ws0 = OH0 - (641 if h == 0 else 513)
                wwid = OH0 + 512 - ws0
                nhalo = max(0, HALO - ws0)
                kw = wwid - 1
                q0 = 513
                qw = wwid - q0
                nvt = kw // 128

                for c in range(8):
                    if l == 0:
                        b.dma("sync", awin[c][:, 0:wwid],
                              self.ubf_dram[128 * c:128 * c + 128, ws0:ws0 + wwid],
                              reads=self.rUbfD, writes=[rAwin[c]])
                    else:
                        if nhalo > 0:
                            b.dma("scalar", awin[c][:, 0:nhalo],
                                  lambda e, c=c, ws0=ws0, nhalo=nhalo:
                                  bass.AP(self.tail_all,
                                          b.env["sc_xrow_halo_b"] + 128 * c * HALO + ws0,
                                          [[HALO, 128], [1, nhalo]]),
                                  reads=[rTailAll], writes=[rAwin[c]],
                                  allow_slow_non_contiguous=(nhalo < 16))
                        b.dma("sync", awin[c][:, nhalo:wwid],
                              self.a2bf_dram[128 * c:128 * c + 128,
                                             ws0 + nhalo - HALO:ws0 + wwid - HALO],
                              reads=[rA2bf[c]], writes=[rAwin[c]])
                        if nhalo > 0:
                            b.op("vector",
                                 lambda e, c=c, nhalo=nhalo:
                                 e.tensor_scalar_mul(awin[c][:, 0:nhalo], awin[c][:, 0:nhalo],
                                                     self.hhf_sb[:, 0:1]),
                                 reads=[self.rConsts, rAwin[c]], writes=[rAwin[c]])

                for (dst, rng0, wdt_t, ti) in ((tq1, q0, qw, 0), (tq2, q0, qw, 1),
                                               (tk1, 1, kw, 2), (tk2, 1, kw, 3)):
                    for cb0 in range(0, wdt_t, 512):
                        cw = min(512, wdt_t - cb0)
                        tc = (BW - 128) * ti + (ws0 + rng0 + cb0 - 128)
                        b.dma("sync", tabst[:, 0:cw], self.tabs_in[:, tc:tc + cw],
                              reads=[self.rConsts, rT], writes=[rTabst])
                        b.op("vector",
                             lambda e, dst=dst, cb0=cb0, cw=cw:
                             e.tensor_copy(out=dst[:, cb0:cb0 + cw], in_=tabst[:, 0:cw]),
                             reads=[rTabst], writes=[rT])

                def hsrc(c, col0, wdt_):
                    if c < 4:
                        return awin[c][:, col0:col0 + wdt_]
                    return awin[c][:, col0 - 1:col0 - 1 + wdt_]

                def emit_wqk_load(ll, woc):
                    b.dma("sync", wqks[woc % 2][:, :],
                          wp.ap3(self.wbf_all, f"wqk{ll}", woc * 1024, 8, 0, 128),
                          reads=[self.rWbfAll], writes=[rWqks[woc % 2]], groups=8)

                if l == 0 and h == 0:
                    emit_wqk_load(l, 0)
                    emit_wqk_load(l, 1)
                for ocg, (strips, rstr, base0, wdt_all) in enumerate(
                        ((qstr, rQs3, q0, qw), (kstr, rKs3, 1, kw))):
                    for oc in range(4):
                        woc = ocg * 4 + oc
                        wbi = woc % 2
                        for cb0 in range(0, wdt_all, 512):
                            cw = min(512, wdt_all - cb0)
                            half = (cb0 // 512) % 2
                            for c in range(8):
                                last = c == 7
                                b.op("tensor",
                                     lambda e, wbi=wbi, c=c, cb0=cb0, cw=cw, half=half,
                                     base0=base0:
                                     e.matmul(PX[:, half * 512:half * 512 + cw],
                                              wqks[wbi][:, c * 128:c * 128 + 128],
                                              hsrc(c, base0 + cb0, cw),
                                              start=(c == 0), stop=(c == 7)),
                                     reads=[rWqks[wbi], rAwin[c]],
                                     war=[rPXh3[half]] if c == 0 else (),
                                     writes=[rPXh3[half]] if last else (), quiet=not last)
                            b.op("scalar",
                                 lambda e, strips=strips, oc=oc, cb0=cb0, cw=cw, half=half:
                                 e.copy(strips[oc][:, cb0:cb0 + cw],
                                        PX[:, half * 512:half * 512 + cw]),
                                 reads=[rPXh3[half]], writes=[rstr[oc]])
                        if woc + 2 < 8:
                            emit_wqk_load(l, woc + 2)
                if not (l == 1 and h == NH - 1):
                    ln_ = l if h < NH - 1 else l + 1
                    emit_wqk_load(ln_, 0)
                    emit_wqk_load(ln_, 1)
                b.op("vector", lambda e: e.memset(v3, 1.0), war=[rV3], writes=[rV3])
                for vt in range(nvt):
                    for c in range(8):
                        last = c == 7
                        b.op("tensor",
                             lambda e, vt=vt, c=c:
                             e.matmul(PX[:, 0:512],
                                      hsrc(c, 1 + vt * 128, 128),
                                      wv_sb[:, c * 512:(c + 1) * 512],
                                      start=(c == 0), stop=(c == 7)),
                             reads=rWv + [rAwin[c]], war=[rPXh3[0]] if c == 0 else (),
                             writes=[rPXh3[0]] if last else (), quiet=not last)
                    b.op("vector",
                         lambda e, vt=vt:
                         e.tensor_copy(out=v3[:, vt * 520:vt * 520 + 520]
                                       .rearrange("p (h x) -> p h x", x=65)[:, :, 0:64],
                                       in_=PX[:, 0:512].rearrange("p (h d) -> p h d", d=64)),
                         reads=[rPXh3[0], rV3], writes=[rV3])

                chains = []
                for ocg, (strips, rstr, wdt_all, t1s, t2s) in enumerate(
                        ((qstr, rQs3, qw, tq1, tq2), (kstr, rKs3, kw, tk1, tk2))):
                    e2 = self.e2qk_sb[l * 2 + ocg]
                    for oc in range(4):
                        for cb0 in range(0, wdt_all, 512):
                            cw = min(512, wdt_all - cb0)
                            chains.append((strips, rstr, oc, cb0, cw, e2, t1s, t2s))
                rST, STt = (rOA3, rOB3), (OA, OB)
                rSL, SLt = (rS13, rS03), (S1, S0)
                rSR, SRt = (rS03, rS13), (S0, S1)

                def l2A(k):
                    strips, rstr, oc, cb0, cw, e2, t1s, t2s = chains[k]
                    p = k % 2
                    b.op("gpsimd",
                         lambda e, strips=strips, oc=oc, cb0=cb0, cw=cw, p=p:
                         e.tensor_tensor(sqp[p][:, 0:cw], strips[oc][:, cb0:cb0 + cw],
                                         strips[oc][:, cb0:cb0 + cw], ALU.mult),
                         reads=[rstr[oc]], writes=[rSqp[p]])
                    b.op("tensor",
                         lambda e, cw=cw, p=p:
                         e.matmul(STt[p][0:2, 0:cw], self.onesb_sb[:, 2:4],
                                  sqp[p][:, 0:cw], start=True, stop=True),
                         reads=[rSqp[p], self.rZero], writes=[rST[p]])

                def l2B(k):
                    strips, rstr, oc, cb0, cw, e2, t1s, t2s = chains[k]
                    p = k % 2
                    mo = p * 512
                    b.op("vector",
                         lambda e, cw=cw, p=p, mo=mo:
                         e.tensor_scalar_max(mrow3[0:2, mo:mo + cw], STt[p][0:2, 0:cw],
                                             1e-24),
                         reads=[rST[p]], writes=[rMr[p]])
                    b.op("scalar",
                         lambda e, cw=cw, p=p, mo=mo:
                         e.activation(srow3[0:2, mo:mo + cw], mrow3[0:2, mo:mo + cw],
                                      AF.Ln, bias=0.0, scale=1.0),
                         reads=[rMr[p], self.rConsts], writes=[rSr[p]])
                    b.op("scalar",
                         lambda e, cw=cw, p=p, mo=mo:
                         e.activation(srow3[0:2, mo:mo + cw], srow3[0:2, mo:mo + cw],
                                      AF.Exp, bias=0.0, scale=-0.5),
                         reads=[rSr[p], self.rConsts], writes=[rSr[p]])
                    b.op("tensor",
                         lambda e, e2=e2, cw=cw, p=p, mo=mo:
                         e.matmul(SLt[p][:, 0:cw], e2[:, :], srow3[0:2, mo:mo + cw],
                                  start=True, stop=True),
                         reads=[rSr[p], self.rConsts], writes=[rSL[p]])

                def l2C(k):
                    strips, rstr, oc, cb0, cw, e2, t1s, t2s = chains[k]
                    p = k % 2
                    b.op("vector",
                         lambda e, strips=strips, oc=oc, cb0=cb0, cw=cw, p=p:
                         e.tensor_tensor(qnp[p][:, 0:cw], strips[oc][:, cb0:cb0 + cw],
                                         SLt[p][:, 0:cw], ALU.mult),
                         reads=[rSL[p], rstr[oc]], writes=[rQnp[p]])
                    b.op("tensor",
                         lambda e, cw=cw, p=p:
                         e.matmul(SRt[p][:, 512:512 + cw], self.pswap_sb[:, :],
                                  qnp[p][:, 0:cw], start=True, stop=True),
                         reads=[rQnp[p], self.rConstsG], writes=[rSR[p]])

                def l2D(k):
                    strips, rstr, oc, cb0, cw, e2, t1s, t2s = chains[k]
                    p = k % 2
                    b.op("gpsimd",
                         lambda e, t1s=t1s, cb0=cb0, cw=cw, p=p:
                         e.tensor_tensor(rotf[p][:, 0:cw], qnp[p][:, 0:cw],
                                         t1s[:, cb0:cb0 + cw], ALU.mult),
                         reads=[rQnp[p], rT], writes=[rRot[p]])
                    b.op("vector",
                         lambda e, t2s=t2s, cb0=cb0, cw=cw, p=p:
                         e.tensor_tensor(SRt[p][:, 512:512 + cw],
                                         SRt[p][:, 512:512 + cw],
                                         t2s[:, cb0:cb0 + cw], ALU.mult),
                         reads=[rT], writes=[rSR[p]])
                    b.op("vector",
                         lambda e, strips=strips, oc=oc, cb0=cb0, cw=cw, p=p:
                         e.tensor_tensor(strips[oc][:, cb0:cb0 + cw], rotf[p][:, 0:cw],
                                         SRt[p][:, 512:512 + cw], ALU.add),
                         reads=[rRot[p], rSR[p]], writes=[rstr[oc]])

                nch = len(chains)
                for i in range(nch + 3):
                    if i < nch:
                        l2A(i)
                    if 1 <= i < nch + 1:
                        l2B(i - 1)
                    if 2 <= i < nch + 2:
                        l2C(i - 2)
                    if 3 <= i:
                        l2D(i - 3)

                tq_blocks = []
                if h == 0:
                    tq_blocks.append(("edge", q0, 128, 1, 5, "medge"))
                kst = 129 if h == 0 else 1
                tq_blocks.append(("own", q0 + (128 if h == 0 else 0), 512, kst, 8,
                                  "mfirst" if h == 0 else "mown"))
                for kind, tqc, tqw, kst_b, nj, mskname in tq_blocks:
                    mskb = {"medge": self.medgeb_sb, "mfirst": self.mownb_sb,
                            "mown": self.mownb_sb}[mskname]
                    mwid = 128 if mskname == "medge" else TSB
                    pend_norm = []

                    def emit_av3(j, hp, nj=nj, tqw=tqw, kst_b=kst_b):
                        vt = (kst_b - 1) // 128 + j
                        lastj = j == nj - 1
                        for hh in range(2):
                            tgt, rt = (OA, rOA3) if hh == 0 else (OB, rOB3)
                            rsel = rPtselA[j % 2] if hh == 0 else rPtselB[j % 2]
                            b.op("tensor",
                                 lambda e, j=j, hh=hh, hp=hp, vt=vt, tgt=tgt,
                                 nj=nj, tqw=tqw:
                                 e.matmul(tgt[0:65, 0:tqw],
                                          v3[:, vt * 520 + (hp * 2 + hh) * 65:
                                             vt * 520 + (hp * 2 + hh) * 65 + 65],
                                          ptsel[j % 2][:, hh * 512:hh * 512 + tqw],
                                          start=(j == 0), stop=(j == nj - 1)),
                                 reads=[rsel, rV3],
                                 war=[rt] if j == 0 else (),
                                 writes=[rt] if lastj else (), quiet=not lastj)

                    def mk_norm_tail(hp, tqw=tqw):
                        def go():
                            for hh in range(2):
                                b.op("scalar",
                                     lambda e, hh=hh, tqw=tqw:
                                     e.activation(srow3[0:1, hh * 512:hh * 512 + tqw],
                                                  otq[hh][64:65, 0:tqw], AF.Ln,
                                                  bias=0.0, scale=1.0),
                                     reads=[rOtq[hh], self.rConsts], writes=[rDinv3])
                            for hh in range(2):
                                b.op("scalar",
                                     lambda e, hh=hh, tqw=tqw:
                                     e.activation(srow3[0:1, hh * 512:hh * 512 + tqw],
                                                  srow3[0:1, hh * 512:hh * 512 + tqw],
                                                  AF.Exp, bias=0.0, scale=-1.0),
                                     reads=[rDinv3, self.rConsts], writes=[rDinv3])
                            b.op("tensor",
                                 lambda e, tqw=tqw:
                                 e.matmul(PX[0:64, 0:tqw], self.ones_sb[0:1, 0:64],
                                          srow3[0:1, 0:tqw], start=True, stop=True),
                                 reads=[rDinv3, self.rConsts], war=[rPXh3[0]], quiet=True)
                            b.op("tensor",
                                 lambda e, tqw=tqw:
                                 e.matmul(PX[0:64, 512:512 + tqw], self.ones_sb[0:1, 0:64],
                                          srow3[0:1, 512:512 + tqw], start=True, stop=True),
                                 reads=[rDinv3, self.rConsts], war=[rPXh3[1]],
                                 writes=[rPXh3[0], rPXh3[1]])
                            for hh in range(2):
                                b.op("vector",
                                     lambda e, hh=hh, hp=hp, tqw=tqw:
                                     e.tensor_tensor(olq[hp][hh * 64:(hh + 1) * 64, 0:tqw],
                                                     otq[hh][0:64, 0:tqw],
                                                     PX[0:64, hh * 512:hh * 512 + tqw],
                                                     ALU.mult),
                                     reads=[rOtq[hh], rPXh3[0], rPXh3[1]],
                                     writes=[rOlq[hp]])
                        return go

                    for hp in range(4):
                        for j in range(nj):
                            ps, rps = (S0, rS03) if j % 2 == 0 else (S1, rS13)
                            kcol = kst_b + 128 * j - 1
                            for hh in range(2):
                                lastm = hh == 1
                                b.op("tensor",
                                     lambda e, hp=hp, j=j, hh=hh, tqc=tqc, tqw=tqw,
                                     kcol=kcol, ps=ps:
                                     e.matmul(ps[:, hh * 512:hh * 512 + tqw],
                                              kstr[hp][hh * 64:(hh + 1) * 64,
                                                       kcol:kcol + 128],
                                              qstr[hp][hh * 64:(hh + 1) * 64,
                                                       tqc - q0:tqc - q0 + tqw],
                                              start=True, stop=True,
                                              tile_position=(hh * 64, 0)),
                                     reads=[rKs3[hp], rQs3[hp]],
                                     war=[rps] if hh == 0 else (),
                                     writes=[rps] if lastm else (), quiet=not lastm)
                            b.op("scalar",
                                 lambda e, j=j, ps=ps, tqw=tqw:
                                 e.activation(pt3[j % 2][:, :].rearrange(
                                     "p (a t) -> p a t", a=2)[:, :, 0:tqw],
                                     ps[:, :].rearrange("p (a t) -> p a t", a=2)[:, :, 0:tqw],
                                     AF.Exp, bias=0.0, scale=QK_SCALE),
                                 reads=[rps, self.rConsts], writes=[rPt3[j % 2]])
                            b.op("vector",
                                 lambda e, j=j, tqw=tqw, mskb=mskb, mwid=mwid:
                                 e.tensor_tensor(ptsel[j % 2][:, 0:tqw],
                                                 pt3[j % 2][:, 0:tqw],
                                                 mskb[:, j * mwid:j * mwid + tqw],
                                                 ALU.mult),
                                 reads=[self.rConsts, rPt3[j % 2]],
                                 writes=[rPtselA[j % 2]])
                            b.op("gpsimd",
                                 lambda e, j=j, tqw=tqw, mskb=mskb, mwid=mwid:
                                 e.tensor_tensor(ptsel[j % 2][:, 512:512 + tqw],
                                                 pt3[j % 2][:, 512:512 + tqw],
                                                 mskb[:, j * mwid:j * mwid + tqw],
                                                 ALU.mult),
                                 reads=[self.rConsts, rPt3[j % 2]],
                                 writes=[rPtselB[j % 2]])
                            if mskname == "mfirst" and j < 4:
                                b.op("vector",
                                     lambda e, j=j, tqw=tqw:
                                     e.tensor_scalar_mul(
                                         ptsel[j % 2].rearrange("p (a t) -> p a t",
                                                                a=2)[:, :, 0:tqw],
                                         ptsel[j % 2].rearrange("p (a t) -> p a t",
                                                                a=2)[:, :, 0:tqw],
                                         self.hhf_sb[:, 0:1]),
                                     reads=[self.rConsts],
                                     writes=[rPtselA[j % 2], rPtselB[j % 2]])
                            if j >= 1:
                                emit_av3(j - 1, hp)
                            if j == 1 and pend_norm:
                                pend_norm.pop(0)()
                        emit_av3(nj - 1, hp)
                        for hh, (ps2, rps2) in enumerate(((OA, rOA3), (OB, rOB3))):
                            b.op("scalar",
                                 lambda e, hh=hh, ps2=ps2, tqw=tqw:
                                 e.copy(otq[hh][0:65, 0:tqw], ps2[0:65, 0:tqw]),
                                 reads=[rps2], writes=[rOtq[hh]])
                        pend_norm.append(mk_norm_tail(hp))
                    while pend_norm:
                        pend_norm.pop(0)()
                    if kind == "own":
                        for oc in range(8):
                            b.dma("sync", af[oc][:, :],
                                  (self.u_dram if l == 0 else self.a2_dram)
                                  [128 * oc:128 * oc + 128, OH0 - HALO:OH0 - HALO + 512],
                                  reads=self.rUdram if l == 0 else [self.rA2dram[oc]],
                                  writes=[rAf[oc]])
                    for oc in range(8):
                        half = oc % 2
                        for ch in range(4):
                            last = ch == 3
                            b.op("tensor",
                                 lambda e, oc=oc, ch=ch, half=half, tqw=tqw:
                                 e.matmul(PX[:, half * 512:half * 512 + tqw],
                                          wo_sb[:, ch * 1024 + oc * 128:
                                                ch * 1024 + oc * 128 + 128],
                                          olq[ch][:, 0:tqw],
                                          start=(ch == 0), stop=(ch == 3)),
                                 reads=rWo + rOlq, war=[rPXh3[half]] if ch == 0 else (),
                                 writes=[rPXh3[half]] if last else (), quiet=not last)
                        if kind == "own":
                            b.op("vector",
                                 lambda e, oc=oc, half=half:
                                 e.tensor_tensor(xm[oc][:, 1:513], PX[:, half * 512:
                                                                      half * 512 + 512],
                                                 af[oc][:, :], ALU.add),
                                 reads=[rPXh3[half], rAf[oc]], writes=[rXm[oc]])
                        else:
                            b.op("vector",
                                 lambda e, oc=oc, half=half:
                                 e.tensor_tensor(xme[oc][:, :], PX[:, half * 512:
                                                                   half * 512 + 128],
                                                 awin[oc][:, q0:q0 + 128], ALU.add),
                                 reads=[rPXh3[half], rAwin[oc]], writes=[rXme[oc]])
                    if kind == "edge":
                        for oc in range(8):
                            b.op("vector",
                                 lambda e, oc=oc:
                                 e.tensor_copy(out=xedge[:, oc:oc + 1],
                                               in_=xme[oc][:, 127:128]),
                                 reads=[rXme[oc]], writes=[rXedge])
                for oc in range(8):
                    b.op("vector",
                         lambda e, oc=oc: e.tensor_copy(out=xm[oc][:, 0:1],
                                                        in_=xedge[:, oc:oc + 1]),
                         reads=[rXedge], writes=[rXm[oc]])
                for oc in range(8):
                    b.op("vector",
                         lambda e, oc=oc: e.tensor_copy(out=xedge[:, oc:oc + 1],
                                                        in_=xm[oc][:, 512:513]),
                         reads=[rXm[oc]], writes=[rXedge])

                def xsh(c):
                    return xm[c][:, 1:513] if c < 4 else xm[c][:, 0:512]
                for c in range(8):
                    p = c % 2
                    b.op("vector",
                         lambda e, c=c, p=p: e.tensor_copy(out=qnp[p][:, :], in_=xsh(c)),
                         reads=[rXm[c]], writes=[rQnp[p]])
                    b.op("gpsimd",
                         lambda e, p=p: e.tensor_tensor(sqp[p][:, :], qnp[p][:, :],
                                                        qnp[p][:, :], ALU.mult),
                         reads=[rQnp[p]], writes=[rSqp[p]])
                    b.op("tensor",
                         lambda e, c=c, p=p: e.matmul(OA[0:1, 0:512], self.onesb_sb[:, 1:2],
                                                      qnp[p][:, :], start=(c == 0),
                                                      stop=(c == 7)),
                         reads=[rQnp[p], self.rZero], war=[rOA3] if c == 0 else (),
                         writes=[rOA3])
                    b.op("tensor",
                         lambda e, c=c, p=p: e.matmul(OB[0:1, 0:512], self.onesb_sb[:, 1:2],
                                                      sqp[p][:, :], start=(c == 0),
                                                      stop=(c == 7)),
                         reads=[rSqp[p], self.rZero], war=[rOB3] if c == 0 else (),
                         writes=[rOB3])
                self._ln_rows(512, rOA3, rOB3, rS13, mrow3, srow3, rMrow3, rSrow3)
                for c in range(8):
                    t, rt_ = (gaf, rGaf) if c % 2 == 0 else (ggf, rGgf)
                    b.op("vector",
                         lambda e, c=c, t=t: e.tensor_tensor(t[:, :], xsh(c), S1[:, 0:512],
                                                             ALU.mult),
                         reads=[rS13, rXm[c]], writes=[rt_])
                    b.op("vector",
                         lambda e, t=t: e.tensor_tensor(t[:, :], t[:, :], S1[:, 512:1024],
                                                        ALU.add),
                         reads=[rS13], writes=[rt_])
                    b.op("vector",
                         lambda e, c=c, l=l, t=t:
                         e.tensor_scalar(hln[c][:, :], t[:, :],
                                         self.b2lgb_sb[:, l * 24 + c:l * 24 + c + 1],
                                         self.b2lgb_sb[:, l * 24 + 8 + c:l * 24 + 8 + c + 1],
                                         ALU.mult, ALU.add),
                         reads=[rt_, self.rConsts], writes=[rHln[c]])

                for hc in range(NHC):
                    bi = hc % 2
                    b.dma("sync", w1s[bi][:, 0:2048],
                          wp.ap3(self.wbf_all, f"w1ag{l}", hc * 1024, 8, 0, 256),
                          reads=[self.rWbfAll], writes=[rW1s[bi]], groups=8)
                    for ag in range(2):
                        for c in range(8):
                            last = c == 7
                            b.op("tensor",
                                 lambda e, hc=hc, bi=bi, ag=ag, c=c:
                                 e.matmul(PX[:, ag * 512:(ag + 1) * 512],
                                          w1s[bi][:, c * 256 + ag * 128:
                                                  c * 256 + ag * 128 + 128],
                                          hln[c][:, :], start=(c == 0), stop=(c == 7)),
                                 reads=[rW1s[bi], rHln[c]],
                                 war=[rPXh3[ag]] if c == 0 else (),
                                 writes=[rPXh3[ag]] if last else (), quiet=not last)
                    b.op("scalar",
                         lambda e, hc=hc, l=l:
                         e.activation(gaf[:, :], PX[:, 0:512], AF.Identity,
                                      bias=self.b1_sb[:, l * 2 * NHC + hc:
                                                      l * 2 * NHC + hc + 1], scale=1.0),
                         reads=[rPXh3[0], self.rConsts], war=[rGaf], writes=[rGaf])
                    b.op("scalar",
                         lambda e, hc=hc, l=l:
                         e.activation(ggf[:, :], PX[:, 512:1024], AF.Gelu,
                                      bias=self.b1_sb[:, l * 2 * NHC + NHC + hc:
                                                      l * 2 * NHC + NHC + hc + 1], scale=1.0),
                         reads=[rPXh3[1], self.rConsts], war=[rGgf], writes=[rGgf])
                    b.op("vector",
                         lambda e, hc=hc:
                         e.tensor_tensor(gg[:, hc * 512:(hc + 1) * 512], gaf[:, :],
                                         ggf[:, :], ALU.mult),
                         reads=[rGaf, rGgf], writes=[rGg])

                W2BANKS = [(S0, 0, rS03), (S0, 512, rS03), (S1, 0, rS13), (S1, 512, rS13),
                           (OA, 0, rOA3), (OB, 0, rOB3), (PX, 0, rPXh3[0]),
                           (PX, 512, rPXh3[1])]
                for hc in range(NHC):
                    bi = hc % 2
                    b.dma("sync", w2s[bi][:, :],
                          wp.ap(self.wbf_all, f"w2{l}", 128 * hc, 128 * hc + 128, 0, D),
                          reads=[self.rWbfAll], writes=[rW2s[bi]])
                    for oc in range(8):
                        ps4, co, rps4 = W2BANKS[oc]
                        b.op("tensor",
                             lambda e, hc=hc, bi=bi, oc=oc, ps4=ps4, co=co:
                             e.matmul(ps4[oc // 6 * 0:128, co:co + 512]
                                      if False else ps4[:, co:co + 512],
                                      w2s[bi][:, oc * 128:(oc + 1) * 128],
                                      gg[:, hc * 512:(hc + 1) * 512],
                                      start=(hc == 0), stop=(hc == NHC - 1)),
                             reads=[rW2s[bi], rGg],
                             war=[rps4] if hc == 0 else (),
                             writes=[rps4])
                for oc in range(8):
                    ps4, co, rps4 = W2BANKS[oc]
                    b.op("scalar",
                         lambda e, oc=oc, ps4=ps4, co=co, l=l:
                         e.activation(gaf[:, :] if False else af[oc][:, :],
                                      ps4[:, co:co + 512], AF.Identity,
                                      bias=self.b2lgb_sb[:, l * 24 + 16 + oc:
                                                         l * 24 + 16 + oc + 1], scale=1.0),
                         reads=[rps4, self.rConsts], war=[rAf[oc]], writes=[rAf[oc]])
                    b.op("vector",
                         lambda e, oc=oc:
                         e.tensor_tensor(xm[oc][:, 1:513], xm[oc][:, 1:513], af[oc][:, :],
                                         ALU.add),
                         reads=[rAf[oc], rXm[oc]], writes=[rXm[oc]])
                if l == 0:
                    for oc in range(8):
                        b.op("vector",
                             lambda e, oc=oc: e.tensor_copy(out=a2bst[:, :],
                                                            in_=xm[oc][:, 1:513]),
                             reads=[rXm[oc]], war=[rA2bst], writes=[rA2bst])
                        b.dma("sync", self.a2bf_dram[128 * oc:128 * oc + 128,
                                                     OH0 - HALO:OH0 - HALO + 512],
                              a2bst[:, :], reads=[rA2bst], writes=[rA2bf[oc]])
                        b.dma("sync", self.a2_dram[128 * oc:128 * oc + 128,
                                                   OH0 - HALO:OH0 - HALO + 512],
                              xm[oc][:, 1:513], reads=[rXm[oc]], writes=[self.rA2dram[oc]])
                else:
                    for oc in range(8):
                        b.dma("sync", af[oc][:, :],
                              self.u_dram[128 * oc:128 * oc + 128, OH0 - HALO:OH0 - HALO + 512],
                              reads=self.rUdram, writes=[rAf[oc]])
                        b.op("vector",
                             lambda e, oc=oc:
                             e.tensor_tensor(xm[oc][:, 1:513], xm[oc][:, 1:513],
                                             af[oc][:, :], ALU.add),
                             reads=[rAf[oc], rXm[oc]], writes=[rXm[oc]])
                        p = oc % 2
                        b.op("vector",
                             lambda e, oc=oc, p=p: e.tensor_copy(out=qnp[p][:, :],
                                                                 in_=xm[oc][:, 1:513]),
                             reads=[rXm[oc]], writes=[rQnp[p]])
                        b.op("gpsimd",
                             lambda e, p=p: e.tensor_tensor(sqp[p][:, :], qnp[p][:, :],
                                                            qnp[p][:, :], ALU.mult),
                             reads=[rQnp[p]], writes=[rSqp[p]])
                        b.op("tensor",
                             lambda e, oc=oc, p=p: e.matmul(OA[0:1, 0:512],
                                                            self.onesb_sb[:, 1:2],
                                                            qnp[p][:, :], start=(oc == 0),
                                                            stop=(oc == 7)),
                             reads=[rQnp[p], self.rZero], war=[rOA3] if oc == 0 else (),
                             writes=[rOA3])
                        b.op("tensor",
                             lambda e, oc=oc, p=p: e.matmul(OB[0:1, 0:512],
                                                            self.onesb_sb[:, 1:2],
                                                            sqp[p][:, :], start=(oc == 0),
                                                            stop=(oc == 7)),
                             reads=[rSqp[p], self.rZero], war=[rOB3] if oc == 0 else (),
                             writes=[rOB3])
                    self._ln_rows(512, rOA3, rOB3, rS13, mrow3, srow3, rMrow3, rSrow3)
                    for oc in range(8):
                        t, rt_ = (gaf, rGaf) if oc % 2 == 0 else (ggf, rGgf)
                        b.op("vector",
                             lambda e, oc=oc, t=t:
                             e.tensor_tensor(t[:, :], xm[oc][:, 1:513], S1[:, 0:512],
                                             ALU.mult),
                             reads=[rS13, rXm[oc]], writes=[rt_])
                        b.op("vector",
                             lambda e, oc=oc, t=t:
                             e.tensor_tensor(af[oc][:, :], t[:, :], S1[:, 512:1024],
                                             ALU.add),
                             reads=[rS13, rt_], war=[rAf[oc]], writes=[rAf[oc]])
                    for tt in range(4):
                        for c in range(8):
                            last = c == 7
                            b.op("tensor",
                                 lambda e, tt=tt, c=c:
                                 e.transpose(PX[:, c * 128:(c + 1) * 128],
                                             af[c][:, tt * 128:(tt + 1) * 128],
                                             self.id_sb[:, :]),
                                 reads=[rAf[c], self.rConsts],
                                 war=[rPXh3[0], rPXh3[1]] if c == 0 else (),
                                 writes=[rPXh3[0], rPXh3[1]] if last else (),
                                 quiet=not last)
                        b.op("scalar",
                             lambda e: e.copy(ytm[:, :], PX[:, :]),
                             reads=[rPXh3[0], rPXh3[1]], war=[rYtm], writes=[rYtm])
                        b.dma("sync", self.y_out[OH0 - HALO + tt * 128:
                                                 OH0 - HALO + (tt + 1) * 128, :],
                              ytm[:, :], reads=[rYtm], writes=[rY])

            if l == 0:
                for r0 in range(0, D, 128):
                    b.dma("sync", self.tail_in[r0:r0 + 128, :],
                          self.a2bf_dram[r0:r0 + 128, TPC - HALO:TPC],
                          reads=[rA2bf[r0 // 128]], writes=[rTailIn])
                b.op("gpsimd",
                     lambda e: e.collective_compute("AllGather", ALU.bypass,
                                                    replica_groups=GROUPS,
                                                    ins=[self.tail_in.ap().opt()],
                                                    outs=[self.tail_all.ap().opt()]),
                     reads=[rTailIn], writes=[rTailAll], coll=True)

        self.abf.reset(); self.af32.reset()

    def _finish(self):
        b = self.b
        tp = self.taps
        def cp(dst, src, rsrc):
            rr = Res(stream=True)
            n = dst.shape[0]
            for r0 in range(0, n, 128):
                b.dma("sync", dst[r0:r0 + 128, :], src[r0:r0 + 128, :],
                      reads=[rsrc] if isinstance(rsrc, Res) else rsrc, writes=[rr])
        if "xfm" in tp:
            cp(tp["xfm"], self.xfm_all, self.rXfmAll)
        if "o" in tp and hasattr(self, "rOAll"):
            cp(tp["o"], self.o_all, self.rOAll)
        if "u" in tp and hasattr(self, "rUdram"):
            cp(tp["u"], self.u_dram, self.rUdram)
        if "a2" in tp and hasattr(self, "rA2dram"):
            for i in range(8):
                cp(tp["a2"][128 * i:128 * i + 128], self.a2_dram[128 * i:128 * i + 128],
                   self.rA2dram[i])
        waits = [(s, b.now(s)) for s in list(b.cnt)]
        b.op("sync", None, extra_wait=waits)



_PROG_CACHE = {}


def _build_in_maps(cfg, wpack, vals):
    SEQ, TPC, BW = cfg.SEQ, cfg.TPC, cfg.BW
    x = np.ascontiguousarray(vals["L0_attn_x"], dtype=np.float32)
    Wq, Wk, Wv = vals["L0_attn_Wq"], vals["L0_attn_Wk"], vals["L0_attn_Wv"]
    wall_vals = {"Wp": vals["L0_attn_Wp"]}
    for l in range(2):
        wqkv = np.asarray(vals["L0_ffn_wqkv"][l], np.float32)
        wall_vals[f"wqk{l}"] = np.ascontiguousarray(
            wqkv[:, :1024].reshape(8, 128, 8, 128).transpose(2, 0, 1, 3)
            .reshape(8 * 8 * 128, 128))
        wall_vals[f"wv{l}"] = np.ascontiguousarray(wqkv[:, 1024:1536])
        wall_vals[f"wo{l}"] = vals["L0_ffn_wo"][l]
        w1 = vals["L0_ffn_w1"][l]
        w1a = np.zeros((D, HIDP), np.float32); w1a[:, :HID] = w1[:, :HID]
        w1g = np.zeros((D, HIDP), np.float32); w1g[:, :HID] = w1[:, HID:]
        tile = lambda w: w.reshape(8, 128, NHC, 128).transpose(2, 0, 1, 3)
        w1ag = np.concatenate([tile(w1a), tile(w1g)], axis=-1)
        w2 = np.zeros((HIDP, D), np.float32); w2[:HID] = vals["L0_ffn_w2"][l]
        wall_vals[f"w1ag{l}"], wall_vals[f"w2{l}"] = \
            np.ascontiguousarray(w1ag.reshape(NHC * 8 * 128, 256)), w2
    wall_flat = wpack.build_flat(wall_vals)
    WR = wpack.rows // 8

    constsv = np.tile(np.array([0.0, 1.0, -CTOP, EPS, 1.0 / D, -1.0, 1e-12, 0.5],
                               np.float32), (128, 1))
    identv = np.eye(128, dtype=np.float32)
    onesv = np.zeros((128, 132), np.float32)
    onesv[:, 0] = 1.0
    onesv[0, :] = 1.0
    onesv[:, 1] = 1.0
    onesv[0:64, 2] = 1.0
    onesv[64:128, 3] = 1.0
    e2v = np.zeros((2, 128), np.float32)
    e2v[0, 0:64] = 1.0
    e2v[1, 64:128] = 1.0
    sigma = np.concatenate([np.arange(32, 64), np.arange(0, 32)])
    psw = np.zeros((128, 128), np.float32)
    for i in range(128):
        h, d0 = divmod(i, 64)
        psw[i, h * 64 + sigma[d0]] = 1.0
    bpv = _chunk_cols(np.asarray(vals["L0_attn_bp"]))
    b1v = np.zeros((128, 2 * 2 * NHC), np.float32)
    for l in range(2):
        b1 = np.asarray(vals["L0_ffn_b1"][l])
        b1a = np.zeros(HIDP, np.float32); b1a[:HID] = b1[:HID]
        b1g = np.zeros(HIDP, np.float32); b1g[:HID] = b1[HID:]
        b1v[:, l * 2 * NHC:l * 2 * NHC + NHC] = _chunk_cols(b1a)
        b1v[:, l * 2 * NHC + NHC:(l + 1) * 2 * NHC] = _chunk_cols(b1g)
    b2lgbv = np.zeros((128, 2 * 3 * 8), np.float32)
    for l in range(2):
        b2lgbv[:, l * 24:l * 24 + 8] = _chunk_cols(np.asarray(vals["L0_ffn_ln_g"][l]))
        b2lgbv[:, l * 24 + 8:l * 24 + 16] = _chunk_cols(np.asarray(vals["L0_ffn_ln_b"][l]))
        b2lgbv[:, l * 24 + 16:l * 24 + 24] = _chunk_cols(np.asarray(vals["L0_ffn_b2"][l]))
    qsksv = np.zeros((128, 8), np.float32)
    for l in range(2):
        qs = np.asarray(vals["L0_ffn_q_scale"][l]); ks = np.asarray(vals["L0_ffn_k_scale"][l])
        qsksv[:, l * 4 + 0] = np.tile(qs, 2)
        qsksv[:, l * 4 + 1] = np.tile(qs[sigma], 2)
        qsksv[:, l * 4 + 2] = np.tile(ks, 2)
        qsksv[:, l * 4 + 3] = np.tile(ks[sigma], 2)
    mtopv = _masks_top()
    tabs_full = _rotary_tables(SEQ)

    in_maps = []
    for c in range(8):
        bidx, gi = c // 4, c % 4
        t0 = gi * TPC
        xe = np.zeros((BW, D), np.float32)
        lo_t = max(0, t0 - HALO)
        xe[BW - (TPC + t0 - lo_t):] = x[bidx, lo_t:t0 + TPC]
        wqv = np.ascontiguousarray(
            Wq[gi * HPC:(gi + 1) * HPC].transpose(1, 0, 2).reshape(D, HPC * DH))
        wkv = np.ascontiguousarray(
            Wk[gi * HPC:(gi + 1) * HPC].transpose(1, 0, 2).reshape(D, HPC * DH))
        wvv = np.ascontiguousarray(
            Wv[gi * HPC:(gi + 1) * HPC].transpose(1, 0, 2).reshape(D, HPC * DH))
        tabs = np.zeros((128, 4 * (BW - 128)), np.float32)
        for ti in range(4):
            sl = tabs_full[ti][:, t0 + 128: t0 + BW]
            tabs[:, ti * (BW - 128):(ti + 1) * (BW - 128)] = np.tile(sl, (2, 1))
        mo, mf, me = _masks_loc(gi)
        ofs = np.zeros((1, 8), np.int32)
        ofs[0, 0] = t0
        ofs[0, 1] = max(t0 - HALO, 0)
        ofs[0, 2] = max(gi - 1, 0) * D
        hhf = np.full((128, 1), 1.0 if gi > 0 else 0.0, np.float32)
        e2qk = np.zeros((8, 128), np.float32)
        for l in range(2):
            for qk in range(2):
                v = np.asarray(vals["L0_ffn_q_scale" if qk == 0 else "L0_ffn_k_scale"][l])
                for h_ in range(2):
                    e2qk[(l * 2 + qk) * 2 + h_, h_ * 64:(h_ + 1) * 64] = v
        in_maps.append(dict(
            x_ext=xe, wq_my=wqv, wk_my=wkv, wv_my=wvv,
            wall_my=np.ascontiguousarray(wall_flat[c * WR:(c + 1) * WR]),
            consts=constsv, ident=identv, ones_in=onesv, e2_in=e2v, pswap_in=psw,
            bp_in=bpv, b1_in=b1v, b2lgb_in=b2lgbv, qsks_in=qsksv, tabs_in=tabs,
            mtop_in=mtopv, ofs_in=ofs,
            mownb_in=mo.astype(BF16NP), medgeb_in=me.astype(BF16NP),
            hhf_in=hhf, e2qk_in=e2qk,
        ))
    return in_maps


def run(cfg, vals, taps=()):
    cfg.taps = set(taps)
    wpack = make_wpack()
    key = (cfg.SEQ, tuple(sorted(cfg.taps)), tuple(sorted(cfg.phases)))
    if key not in _PROG_CACHE:
        _PROG_CACHE[key] = Prog(cfg, wpack)
    prog = _PROG_CACHE[key]
    in_maps = _build_in_maps(cfg, wpack, vals)
    res = run_bass_kernel_spmd(prog.nc, in_maps, core_ids=list(range(8)))
    return prog, res


def kernel(**inputs):
    cfg = Cfg(8192)
    prog, res = run(cfg, inputs)
    TPC = cfg.TPC
    y = np.zeros((2, 8192, D), np.float32)
    for c in range(8):
        bidx, gi = c // 4, c % 4
        y[bidx, gi * TPC:(gi + 1) * TPC] = res.results[c]["y_out"]
    return y
